# version 05
# baseline (speedup 1.0000x reference)
import numpy as np
from contextlib import ExitStack

import concourse.bass as bass
import concourse.mybir as mybir
from concourse.bass_utils import run_bass_kernel_spmd

F32 = mybir.dt.float32
F32R = mybir.dt.float32r
F16 = mybir.dt.float16
ALU = mybir.AluOpType
AF = mybir.ActivationFunctionType
AX = mybir.AxisListType
ET = mybir.EngineType

ATTN_SUB = 5
LN_SUB = 4
SETUP_SUB = 2
BARRIER_OFF = 0
DBG = 0

NCORES = 8
D = 2048
NK = 16
TCORE = 2048
TP = 1024
NPASS = 2
NE = 32
EPAIRS = 6
EDK = 6
ETILES = EPAIRS * 32 + 16 * EDK
CH = 16
ECH = ETILES // CH
SPW = 512
SPN = 8
SPAIRS = SPW // 128
SDK = SPW // 128
STILES = SPAIRS * 32 + 16 * SDK
SCH = STILES // CH
NCB = 6

SEMS = ("sX", "sW", "sCK", "sCB", "sPE", "sACT", "sDVE")
ENGS = ("sync", "gpsimd", "tensor", "scalar", "vector")


class Plan:
    def __init__(self):
        self.ops = {e: [] for e in ENGS}
        self.meta = {e: [] for e in ENGS}
        self.c = {s: 0 for s in SEMS}
        self.gchunk = 0
        self.attn_mark = 0
        self.chunk_release = {}
        self.pending_chunks = []
        self.crow_release = []
        self.dbg_done = False
        self.dbg_gate = 0

    def emit(self, eng, fn, meta=None):
        self.ops[eng].append(fn)
        self.meta[eng].append(meta or ("op",))

    def bump(self, sem, n):
        self.c[sem] += n
        return self.c[sem]


def check_deadlock(plan):
    pos = {e: 0 for e in ENGS}
    cnt = {s: 0 for s in SEMS}
    progress = True
    while progress:
        progress = False
        for e in ENGS:
            while pos[e] < len(plan.meta[e]):
                m = plan.meta[e][pos[e]]
                if m[0] == "wait":
                    if cnt[m[1]] < m[2]:
                        break
                elif m[0] == "inc":
                    for s, n in m[1].items():
                        cnt[s] += n
                pos[e] += 1
                progress = True
    stuck = {e: (pos[e], len(plan.meta[e])) for e in ENGS if pos[e] < len(plan.meta[e])}
    if stuck:
        lines = []
        for e, (p, n) in stuck.items():
            m = plan.meta[e][p]
            lines.append(f"{e} stuck at {p}/{n}: {m}; cnt={{{m[1]}: {cnt.get(m[1]) if m[0]=='wait' else '?'}}}")
        raise RuntimeError("DEADLOCK in planned schedule:\n" + "\n".join(lines))
    return cnt


def build_nc(stage=7):
    nc = bass.Bass("TRN2", target_bir_lowering=False, debug=False)
    nc.detect_race_conditions = False

    def din(name, shape, dt):
        return nc.dram_tensor(name, list(shape), dt, kind="ExternalInput").ap()

    d_xT = din("xT", (D, TCORE), F32R)
    d_wexp = din("wexp", (128, NE * ETILES * 128), F32)
    d_wsh = din("wsh", (128, SPN * STILES * 128), F32)
    d_gwt = din("gwt", (128, NK * NE), F32)
    d_segt = din("segt", (128, NK), F32)
    d_biasc = din("biasc", (128, NK), F32)
    d_w0row = din("w0row", (1, D), F32R)
    d_w1row = din("w1row", (1, D), F32R)
    d_d0 = din("d0", (1, 1), F32)
    d_d1 = din("d1", (1, 1), F32)
    d_onescol = din("onescol", (128, 1), F32R)
    d_onesrow = din("onesrow", (1, 128), F32R)
    d_ident = din("ident", (128, 128), F32)
    d_ident16 = din("ident16", (128, 128), F16)
    d_onesrow16 = din("onesrow16", (1, 128), F16)
    d_yT = nc.dram_tensor("yT", [D, TCORE], F32R, kind="ExternalOutput").ap()
    d_dbg = nc.dram_tensor("dbg", [128, 2048], F32, kind="ExternalOutput").ap()

    plan = Plan()

    with ExitStack() as st:
        def sbt(name, shape, dt):
            return st.enter_context(nc.sbuf_tensor(name, list(shape), dt))

        xt = sbt("xt", (128, NK * TP), F32R)
        ub = sbt("ub", (128, NK * TP), F16)
        arena = sbt("arena", (128, 8192), F32R)
        cbufs = sbt("cbufs", (128, 6 * 2048), F16)
        arenaF = sbt("arenaF", (128, 2048), F32)
        h = sbt("h", (128, 6144), F16)
        combT = sbt("combT", (32, TP), F16)
        crow = sbt("crow", (1, 2 * TP), F16)
        scale_bc = sbt("scale_bc", (128, TP), F32)
        gwt = sbt("gwt_s", (128, NK * NE), F16)
        segt = sbt("segt_s", (128, NK), F16)
        biasc = sbt("biasc_s", (128, NK), F32)
        onescol = sbt("onescol_s", (128, 1), F32R)
        onesrow = sbt("onesrow_s", (1, 128), F32R)
        onesrow16 = sbt("onesrow16_s", (1, 128), F16)
        ident = sbt("ident_s", (128, 128), F32)
        ident16 = sbt("ident16_s", (128, 128), F16)
        colsA = sbt("colsA", (128, 8), F32)
        colsB = sbt("colsB", (128, 8), F32)
        colsC = sbt("colsC", (128, 2), F32)
        colsD = sbt("colsD", (128, 8), F32)
        colsS = sbt("colsS", (128, 8), F32)
        dvr = sbt("dvr", (1, 2), F32R)
        dtmp = sbt("dtmp", (1, 2), F32)

        PA = st.enter_context(nc.psum_tensor("PA", [128, 1024], F32))
        PB = st.enter_context(nc.psum_tensor("PB", [128, 1024], F32))
        PC = st.enter_context(nc.psum_tensor("PC", [128, 1024], F32))
        PD = st.enter_context(nc.psum_tensor("PD", [128, 1024], F32))
        PS = [PA, PB, PC, PD]

        sems = {s: st.enter_context(nc.semaphore(s)) for s in SEMS}

        block = st.enter_context(nc.Block())

        def cbuf(b):
            return cbufs[:, 2048 * b:2048 * (b + 1)]
        def xsq(j):
            return arena[:, 6144 + 1024 * j: 6144 + 1024 * (j + 1)]
        def sg(j):
            return arenaF[:, 1024 * j:1024 * (j + 1)]
        ut_cur = arena[:, 0:2048]
        w0bc = arena[:, 2048:4096]
        w1bc = arena[:, 4096:6144]
        yF2 = arenaF[:, 0:2048]
        ltm = arenaF[:, 0:512]
        msk = arenaF[:, 512:544]
        comb_tm = arenaF[:, 544:800]
        rowA = arenaF[0:1, 0:1024]
        rowA2 = arenaF[0:1, 1024:2048]

        def hk(k):
            return h[:, 1024 * k:1024 * (k + 1)]
        lg = h[:, 0:1024]
        m_row = crow[0:1, 1024:2048]
        w0row_sb = arena[0:1, 2048:4096]
        w1row_sb = arena[0:1, 4096:6144]
        rinv_row = crow[0:1, 0:1024]
        recip_tmp = scale_bc[0:1, 0:1024]

        def x_t(m):
            return xt[:, TP * m:TP * (m + 1)]
        def u_t(k):
            return ub[:, TP * k:TP * (k + 1)]

        def W_(eng, sem, v):
            plan.emit(eng, lambda e, sem=sem, v=v: e.wait_ge(sems[sem], v),
                      meta=("wait", sem, v))

        def mm(out_ap, lhs_ap, rhs_ap, start, stop, incs=()):
            def f(e, out_ap=out_ap, lhs_ap=lhs_ap, rhs_ap=rhs_ap,
                  start=start, stop=stop, incs=tuple(incs)):
                ins = e.matmul(out_ap, lhs_ap, rhs_ap, start=start, stop=stop)
                for s, n in incs:
                    ins = ins.then_inc(sems[s], n)
            plan.emit("tensor", f, meta=("inc", dict(incs), "mm"))

        def trans(out_ap, in_ap, id_ap, incs=()):
            def f(e, out_ap=out_ap, in_ap=in_ap, id_ap=id_ap, incs=tuple(incs)):
                ins = e.transpose(out_ap, in_ap, id_ap)
                for s, n in incs:
                    ins = ins.then_inc(sems[s], n)
            plan.emit("tensor", f, meta=("inc", dict(incs), "trans"))

        def pe_inc():
            v = plan.bump("sPE", 1)
            return [("sPE", 1)], v

        def dve(opname, *args, incs=()):
            def f(e, opname=opname, args=args, incs=tuple(incs)):
                try:
                    ins = getattr(e, opname)(*args)
                except Exception as ex:
                    raise RuntimeError(f"DVE op {opname} failed: {ex}") from ex
                for s, n in incs:
                    ins = ins.then_inc(sems[s], n)
            plan.emit("vector", f, meta=("inc", dict(incs), opname))

        def dve_i(opname, *args):
            dve(opname, *args, incs=[("sDVE", 1)])
            return plan.bump("sDVE", 1)

        def act(opname, *args, incs=(), kwargs=None):
            def f(e, opname=opname, args=args, incs=tuple(incs), kwargs=kwargs or {}):
                ins = getattr(e, opname)(*args, **kwargs)
                for s, n in incs:
                    ins = ins.then_inc(sems[s], n)
            plan.emit("scalar", f, meta=("inc", dict(incs), opname))

        def dma(eng, out_ap, in_ap):
            sem = "sX" if eng == "sync" else "sW"
            def f(e, out_ap=out_ap, in_ap=in_ap, sem=sem):
                e.dma_start(out=out_ap, in_=in_ap).then_inc(sems[sem], 16)
            plan.emit(eng, f, meta=("inc", {sem: 16}, "dma"))
            return plan.bump(sem, 16)

        def barrier():
            if BARRIER_OFF:
                return
            sem_of = {"tensor": "sPE", "scalar": "sACT", "vector": "sDVE"}
            tgt = {s: plan.c[s] for s in sem_of.values()}
            for eng, s in sem_of.items():
                for o in sem_of.values():
                    if o != s:
                        W_(eng, o, tgt[o])

        def stream_chunks(src_ap, g0, n_chunks, gate_first=None):
            for j in range(n_chunks):
                g = g0 + j
                if g >= NCB:
                    W_("gpsimd", "sPE", plan.chunk_release[g - (NCB - 1)])
                if j < NCB and gate_first is not None:
                    W_("gpsimd", gate_first[0], gate_first[1])
                dma("gpsimd", cbuf(g % NCB)[:],
                    src_ap[:, 2048 * j:2048 * (j + 1)])

        def layernorm():
            if LN_SUB < 1:
                return
            marks = []
            for k in range(NK):
                if k >= 2:
                    W_("vector", "sPE", marks[k - 2])
                dm = dve_i("tensor_tensor", xsq(k % 2)[:], x_t(k), x_t(k), ALU.mult)
                W_("tensor", "sDVE", dm)
                for n in range(2):
                    mm(PA[0:1, 512 * n:512 * (n + 1)], onescol[:],
                       x_t(k)[:, 512 * n:512 * (n + 1)], k == 0, k == NK - 1)
                incs, mk = pe_inc()
                marks.append(mk)
                for n in range(2):
                    mm(PB[0:1, 512 * n:512 * (n + 1)], onescol[:],
                       xsq(k % 2)[:, 512 * n:512 * (n + 1)], k == 0, k == NK - 1,
                       incs=incs if n == 1 else ())
            if LN_SUB < 2:
                return
            W_("vector", "sPE", plan.c["sPE"])
            dve("tensor_scalar", m_row[:, 0:512], PA[0:1, 0:512], 1.0 / D, None, ALU.mult)
            dve("tensor_scalar", m_row[:, 512:1024], PA[0:1, 512:1024], 1.0 / D, None, ALU.mult)
            dve("tensor_scalar", rowA[:], PB[0:1, :], 1.0 / D, None, ALU.mult)
            dve("tensor_tensor", rowA2[:], m_row[:], m_row[:], ALU.mult)
            dve("tensor_sub", rowA[:], rowA[:], rowA2[:])
            dm = dve_i("tensor_scalar", rowA[:], rowA[:], 1e-5, None, ALU.add)
            W_("scalar", "sDVE", dm)
            act("activation", rowA2[:], rowA[:], AF.Sqrt, incs=[("sACT", 1)])
            am = plan.bump("sACT", 1)
            W_("vector", "sACT", am)
            dve("reciprocal", scale_bc[0:1, 0:1024], rowA2[:])
            dm = dve_i("tensor_copy", rinv_row[:], scale_bc[0:1, 0:1024])
            W_("tensor", "sDVE", dm)
            if LN_SUB < 3:
                return
            for n in range(2):
                mm(PC[:, 512 * n:512 * (n + 1)], onesrow16[:],
                   m_row[:, 512 * n:512 * (n + 1)], True, True)
            incs, bcm = pe_inc()
            for n in range(2):
                mm(PD[:, 512 * n:512 * (n + 1)], onesrow16[:],
                   rinv_row[:, 512 * n:512 * (n + 1)], True, True,
                   incs=incs if n == 1 else ())
            if LN_SUB == 3:
                W_("vector", "sPE", bcm)
                dve_i("tensor_copy", u_t(0)[:, 0:512], PA[:, 0:512])
                return
            if LN_SUB == 7:
                W_("vector", "sPE", bcm)
                dve_i("tensor_copy", u_t(0), PC[:])
                return
            W_("vector", "sPE", bcm)
            for k in range(NK):
                dve("tensor_sub", u_t(k), x_t(k), PC[:])
                dve_i("tensor_tensor", u_t(k), u_t(k), PD[:], ALU.mult)

        def router():
            W_("tensor", "sDVE", plan.c["sDVE"])
            W_("tensor", "sW", 32)
            incs, lgm = pe_inc()
            for k in range(NK):
                for n in range(2):
                    last = (k == NK - 1 and n == 1)
                    mm(PA[0:32, 512 * n:512 * (n + 1)], gwt[:, 32 * k:32 * (k + 1)],
                       u_t(k)[:, 512 * n:512 * (n + 1)], k == 0, k == NK - 1,
                       incs=incs if last else ())
            W_("vector", "sPE", lgm)
            dm = dve_i("tensor_copy", lg[0:32, :], PA[0:32, :])
            W_("tensor", "sDVE", dm)
            incs, trm = pe_inc()
            for c in range(8):
                trans(PB[:, 16 * c:16 * (c + 1)].bitcast(F16),
                      lg[0:32, 128 * c:128 * (c + 1)],
                      ident16[0:32, 0:32], incs=incs if c == 7 else ())
            L3 = ltm[:, 0:256].rearrange("p (c e) -> p c e", e=32)
            C3 = comb_tm[:, 0:256].rearrange("p (c e) -> p c e", e=32)
            M3 = arenaF[:, 800:1056].rearrange("p (c e) -> p c e", e=32)
            cA = colsA[:, 0:8]
            def bc3(cols):
                return cols[:, :, None].broadcast_to([128, 8, 32])
            def pool(opname, *args, incs=()):
                def f(e, opname=opname, args=args, incs=tuple(incs)):
                    ins = getattr(e, opname)(*args)
                    for s, n in incs:
                        ins = ins.then_inc(sems[s], n)
                plan.emit("gpsimd", f, meta=("inc", dict(incs), opname))
            def pool_i(opname, *args):
                pool(opname, *args, incs=[("sCK", 1)])
                return plan.bump("sCK", 1)
            W_("scalar", "sPE", trm)
            act("copy", ltm[:, 0:256], PB[:, 0:128].bitcast(F16), incs=[("sACT", 1)])
            am = plan.bump("sACT", 1)
            W_("vector", "sACT", am)
            dm = dve_i("reduce_max", cA, L3, AX.X)
            W_("gpsimd", "sDVE", dm)
            pm = pool_i("tensor_tensor", C3, L3, bc3(cA), ALU.subtract)
            W_("scalar", "sCK", pm)
            act("activation", ltm[:, 0:256], comb_tm[:, 0:256], AF.Exp,
                incs=[("sACT", 1)])
            am = plan.bump("sACT", 1)
            W_("vector", "sACT", am)
            dm = dve_i("reduce_sum", colsS[:, 0:8], L3, AX.X)
            W_("gpsimd", "sDVE", dm)
            pm = pool_i("tensor_copy", colsB[:, 0:8], colsS[:, 0:8])
            W_("vector", "sCK", pm)
            dm = dve_i("reciprocal", colsD[:, 0:8], colsB[:, 0:8])
            W_("gpsimd", "sDVE", dm)
            pm = pool_i("tensor_tensor", C3, L3, bc3(colsD[:, 0:8]), ALU.mult)
            if DBG and not plan.dbg_done:
                W_("vector", "sCK", pm)
                dve("tensor_copy", scale_bc[:, 0:256], comb_tm[:, 0:256])
            for j in range(4):
                src3 = C3 if j == 0 else L3
                W_("vector", "sCK", plan.c["sCK"])
                dm = dve_i("reduce_max", cA, src3, AX.X)
                W_("gpsimd", "sDVE", dm)
                pm = pool_i("tensor_copy", colsB[:, 0:8], cA)
                if j == 3:
                    break
                W_("vector", "sCK", pm)
                dve("tensor_tensor", M3, src3, bc3(colsB[:, 0:8]), ALU.is_ge)
                dm = dve_i("scalar_tensor_tensor", L3, M3, -1e30, src3,
                           ALU.mult, ALU.add)
            W_("vector", "sCK", pm)
            dve("tensor_tensor", M3, C3, bc3(colsB[:, 0:8]), ALU.is_ge)
            dm = dve_i("tensor_tensor", C3, C3, M3, ALU.mult)
            W_("vector", "sDVE", dm)
            dm = dve_i("reduce_sum", colsS[:, 0:8], C3, AX.X)
            W_("gpsimd", "sDVE", dm)
            pm = pool_i("tensor_copy", colsB[:, 0:8], colsS[:, 0:8])
            W_("vector", "sCK", pm)
            dm = dve_i("reciprocal", colsD[:, 0:8], colsB[:, 0:8])
            W_("gpsimd", "sDVE", dm)
            pm = pool_i("tensor_copy", colsB[:, 0:8], colsD[:, 0:8])
            W_("vector", "sCK", pm)
            dve("tensor_tensor", C3, C3, bc3(colsB[:, 0:8]), ALU.mult)
            dm = dve_i("tensor_scalar", comb_tm[:, 0:256], comb_tm[:, 0:256],
                       0.5, None, ALU.mult)
            W_("tensor", "sDVE", dm)
            incs, ctm = pe_inc()
            for c in range(8):
                trans(PC[0:32, 128 * c:128 * (c + 1)],
                      comb_tm[:, 32 * c:32 * (c + 1)],
                      ident[:, :], incs=incs if c == 7 else ())
            W_("vector", "sPE", ctm)
            return dve_i("tensor_copy", combT[0:32, :], PC[0:32, :])

        def shared_gate():
            W_("tensor", "sDVE", plan.c["sDVE"])
            incs, sgm = pe_inc()
            for k in range(NK):
                for n in range(2):
                    last = (k == NK - 1 and n == 1)
                    mm(PD[0:1, 512 * n:512 * (n + 1)], segt[:, k:k + 1],
                       u_t(k)[:, 512 * n:512 * (n + 1)], k == 0, k == NK - 1,
                       incs=incs if last else ())
            W_("scalar", "sPE", sgm)
            act("activation", rowA[:], PD[0:1, :], AF.Sigmoid, incs=[("sACT", 1)])
            am = plan.bump("sACT", 1)
            W_("vector", "sACT", am)
            dm = dve_i("tensor_scalar", m_row[:], rowA[:], 0.5, None, ALU.mult)
            W_("tensor", "sDVE", dm)
            incs, bcm = pe_inc()
            for n in range(2):
                mm(PB[:, 512 * n:512 * (n + 1)], onesrow16[:],
                   m_row[:, 512 * n:512 * (n + 1)], True, True,
                   incs=incs if n == 1 else ())
            W_("vector", "sPE", bcm)
            if DBG:
                W_("vector", "sX", plan.dbg_gate)
            dve_i("tensor_copy", scale_bc[:], PB[:])

        class Ev:
            pass

        def expert_block(ev, pairs, dkt, do_comb, crow_mark, pe_tile, acc=None):
            if do_comb:
                W_("tensor", "sX", crow_mark)
                W_("tensor", ev.PB_free[0], ev.PB_free[1])
                incs, cm = pe_inc()
                for n in range(2):
                    mm(PB[:, 512 * n:512 * (n + 1)], onesrow16[:],
                       crow[0:1, 512 * n:512 * (n + 1)], True, True,
                       incs=incs if n == 1 else ())
                plan.crow_release.append(cm)
                W_("vector", "sPE", cm)
                ev.PB_free = ("sDVE", dve_i("tensor_copy", scale_bc[:], PB[:]))

            def tile_ap(t):
                g = pe_tile[0] + t
                c = g // CH
                return cbuf(c % NCB)[:, 128 * (g % CH):128 * (g % CH) + 128], c, \
                    (g % CH == CH - 1)

            def release_pending(pe_val):
                for g in plan.pending_chunks:
                    plan.chunk_release[g] = pe_val
                plan.pending_chunks = []

            hready = None
            for p in range(pairs):
                W_("tensor", ev.PA_free[0], ev.PA_free[1])
                incs_g, gmark = pe_inc()
                for k in range(NK):
                    ap, c, islast = tile_ap(p * 32 + k)
                    if (p * 32 + k + pe_tile[0]) % CH == 0:
                        W_("tensor", "sW", 32 + 16 * (c + 1))
                    for n in range(2):
                        extra = incs_g if (k == NK - 1 and n == 1) else []
                        mm(PA[:, 512 * n:512 * (n + 1)], ap,
                           u_t(k)[:, 512 * n:512 * (n + 1)], k == 0, k == NK - 1,
                           incs=extra)
                    if islast:
                        plan.pending_chunks.append(c)
                release_pending(gmark)
                W_("scalar", "sPE", gmark)
                W_("scalar", "sDVE", ev.sg_free[p % 2])
                act("activation", sg(p % 2)[:], PA[:], AF.Silu, incs=[("sACT", 1)])
                ev.PA_free = ("sACT", plan.bump("sACT", 1))
                W_("vector", "sACT", plan.c["sACT"])
                dve("tensor_tensor", sg(p % 2)[:], sg(p % 2)[:], scale_bc[:], ALU.mult)
                W_("tensor", ev.PB_free[0], ev.PB_free[1])
                incs_u, umark = pe_inc()
                for k in range(NK):
                    ap, c, islast = tile_ap(p * 32 + 16 + k)
                    if (p * 32 + 16 + k + pe_tile[0]) % CH == 0:
                        W_("tensor", "sW", 32 + 16 * (c + 1))
                    for n in range(2):
                        extra = incs_u if (k == NK - 1 and n == 1) else []
                        mm(PB[:, 512 * n:512 * (n + 1)], ap,
                           u_t(k)[:, 512 * n:512 * (n + 1)], k == 0, k == NK - 1,
                           incs=extra)
                    if islast:
                        plan.pending_chunks.append(c)
                release_pending(umark)
                W_("vector", "sPE", umark)
                hm = dve_i("tensor_tensor", hk(p), PB[:], sg(p % 2)[:], ALU.mult)
                ev.PB_free = ("sDVE", hm)
                ev.sg_free[p % 2] = hm
                hready = hm
            W_("tensor", "sDVE", hready)
            for m in range(NK):
                reg = m % 2
                P_ = PC if reg == 0 else PD
                if acc != 1:
                    W_("tensor", ev.PCD_free[reg][0], ev.PCD_free[reg][1])
                incs_d, dmark = pe_inc()
                for k in range(dkt):
                    ap, c, islast = tile_ap(pairs * 32 + m * dkt + k)
                    if (pairs * 32 + m * dkt + k + pe_tile[0]) % CH == 0:
                        W_("tensor", "sW", 32 + 16 * (c + 1))
                    for n in range(2):
                        extra = incs_d if (k == dkt - 1 and n == 1) else []
                        mm(P_[:, 512 * n:512 * (n + 1)], ap,
                           hk(k)[:, 512 * n:512 * (n + 1)],
                           (k == 0 and acc != 1),
                           (k == dkt - 1 and acc != 0),
                           incs=extra)
                    if islast:
                        plan.pending_chunks.append(c)
                release_pending(dmark)
                if acc != 0:
                    W_("vector", "sPE", dmark)
                    dm = dve_i("scalar_tensor_tensor", x_t(m), P_[:], 1.0, x_t(m),
                               ALU.mult, ALU.add)
                    ev.PCD_free[reg] = ("sDVE", dm)
            pe_tile[0] += pairs * 32 + NK * dkt

        def fresh_ev():
            ev = Ev()
            ev.PA_free = ("sACT", plan.c["sACT"])
            ev.PB_free = ("sDVE", plan.c["sDVE"])
            ev.PCD_free = [("sDVE", plan.c["sDVE"]), ("sDVE", plan.c["sDVE"])]
            ev.sg_free = {0: plan.c["sDVE"], 1: plan.c["sDVE"]}
            return ev

        def ffn_call():
            barrier()
            layernorm()
            barrier()
            combT_mark = router()
            if DBG and not plan.dbg_done:
                plan.dbg_done = True
                W_("sync", "sDVE", combT_mark)
                dma("sync", d_dbg.bitcast(F32R)[0:32, 0:1024], combT[0:32, :])
                sxm = dma("sync", d_dbg[:, 1024:2048], scale_bc[:, 0:1024])
                plan.dbg_gate = sxm
            barrier()
            shared_gate()
            barrier()
            ch0 = plan.gchunk
            plan.gchunk += SPN * SCH
            ev = fresh_ev()
            pe_tile = [ch0 * CH]
            for j in range(SPN):
                expert_block(ev, SPAIRS, SDK, False, None, pe_tile)
            stream_chunks(d_wsh, ch0, SPN * SCH)
            barrier()
            ch1 = plan.gchunk
            plan.gchunk += NE * ECH
            crow_base = len(plan.crow_release)
            xB = plan.c["sX"]
            crow_marks = [xB + 16 * (e_ + 1) for e_ in range(NE)]
            for e_ in range(NE):
                plan.bump("sX", 16)
            ev = fresh_ev()
            pe_tile = [ch1 * CH]
            for e_ in range(NE):
                expert_block(ev, EPAIRS, EDK, True, crow_marks[e_], pe_tile)
            for e_ in range(NE):
                W_("sync", "sDVE", combT_mark)
                if e_ > 0:
                    W_("sync", "sPE", plan.crow_release[crow_base + e_ - 1])
                def f(e, e_=e_):
                    e.dma_start(out=crow[0:1, 0:1024], in_=combT[e_:e_ + 1, :]).then_inc(
                        sems["sX"], 16)
                plan.emit("sync", f, meta=("inc", {"sX": 16}, "dma"))
            stream_chunks(d_wexp, ch1, NE * ECH)
            barrier()

        def slot(d_):
            P_ = PS[(d_ % 8) // 2]
            off = 512 * (d_ % 2) + 128 * (d_ // 8)
            return P_[:, off:off + 128]

        def slot16(d_):
            P_ = PS[(d_ % 8) // 2]
            off = 512 * (d_ % 2) + 64 * (d_ // 8)
            return P_[:, off:off + 64].bitcast(F16)

        def attn():
            if ATTN_SUB < 1:
                return
            barrier()
            layernorm()
            barrier()
            if ATTN_SUB < 2:
                plan.attn_mark = plan.c["sDVE"]
                return
            W_("sync", "sDVE", plan.c["sDVE"])
            dma("sync", w0row_sb[:], d_w0row)
            r2 = dma("sync", w1row_sb[:], d_w1row)
            W_("tensor", "sX", r2)
            incs, bm = pe_inc()
            for n in range(4):
                mm((PA if n < 2 else PB)[:, 512 * (n % 2):512 * (n % 2) + 512],
                   onesrow[:], w0row_sb[:, 512 * n:512 * (n + 1)], True, True)
            for n in range(4):
                mm((PC if n < 2 else PD)[:, 512 * (n % 2):512 * (n % 2) + 512],
                   onesrow[:], w1row_sb[:, 512 * n:512 * (n + 1)], True, True,
                   incs=incs if n == 3 else ())
            W_("vector", "sPE", bm)
            dve("tensor_copy", w0bc[:, 0:1024], PA[:])
            dve("tensor_copy", w0bc[:, 1024:2048], PB[:])
            dve("tensor_copy", w1bc[:, 0:1024], PC[:])
            dm = dve_i("tensor_copy", w1bc[:, 1024:2048], PD[:])
            W_("tensor", "sDVE", dm)
            if ATTN_SUB < 3:
                plan.attn_mark = plan.c["sDVE"]
                barrier()
                return
            dv0b = colsC[:, 0:1].broadcast_to([128, 2048])
            dv1b = colsC[:, 1:2].broadcast_to([128, 2048])
            scX = arena[:, 6144:8192]
            evs2 = {}
            for tt in range(8):
                tok0 = 128 * tt
                evs = {}
                for d_ in range(NK):
                    if d_ >= 8:
                        W_("tensor", "sDVE", evs[d_ - 8])
                    elif tt > 0:
                        W_("tensor", "sDVE", evs2[d_ + 8])
                    incs, mk = pe_inc()
                    trans(slot16(d_), u_t(d_)[:, tok0:tok0 + 128],
                          ident16[:, :], incs=incs)
                    W_("vector", "sPE", mk)
                    evs[d_] = dve_i("tensor_copy",
                                    ut_cur[:, 128 * d_:128 * (d_ + 1)], slot16(d_))
                if ATTN_SUB < 4:
                    evs2 = {k: v for k, v in evs.items()}
                    continue
                dve("tensor_tensor", scX, ut_cur[:].bitcast(F32),
                    w0bc[:].bitcast(F32), ALU.mult)
                dve("tensor_tensor_scan", yF2[:], dv0b, scX.bitcast(F32),
                    0.0, ALU.mult, ALU.add)
                dve("tensor_copy", scX, yF2[:])
                dve("tensor_tensor_scan", yF2[:], dv1b, ut_cur[:].bitcast(F32),
                    0.0, ALU.mult, ALU.add)
                dve("tensor_tensor", yF2[:], yF2[:], w1bc[:].bitcast(F32), ALU.mult)
                ysc = dve_i("tensor_tensor", yF2[:], yF2[:], scX.bitcast(F32),
                            ALU.add)
                W_("tensor", "sDVE", ysc)
                if ATTN_SUB < 5:
                    evs2 = {k: v for k, v in evs.items()}
                    continue
                evs2 = {}
                for d_ in range(NK):
                    if d_ >= 8:
                        W_("tensor", "sDVE", evs2[d_ - 8])
                    incs, mk = pe_inc()
                    trans(slot(d_), yF2[:, 128 * d_:128 * (d_ + 1)],
                          ident[:, :], incs=incs)
                    W_("vector", "sPE", mk)
                    evs2[d_] = dve_i("scalar_tensor_tensor",
                                     x_t(d_)[:, tok0:tok0 + 128], slot(d_),
                                     biasc[:, d_:d_ + 1],
                                     x_t(d_)[:, tok0:tok0 + 128],
                                     ALU.add, ALU.add)
            plan.attn_mark = plan.c["sDVE"]
            barrier()

        for dst, src in [(biasc[:], d_biasc),
                         (onescol[:], d_onescol), (onesrow[:], d_onesrow),
                         (onesrow16[:], d_onesrow16),
                         (ident[:], d_ident), (ident16[:], d_ident16),
                         (dtmp[0:1, 0:1], d_d0), (dtmp[0:1, 1:2], d_d1)]:
            dma("sync", dst, src)
        setup_mark = plan.c["sX"]
        dma("gpsimd", gwt[:], d_gwt)
        gseg_mark = dma("gpsimd", segt[:], d_segt)
        if SETUP_SUB >= 1:
            W_("vector", "sX", setup_mark)
            dm = dve_i("tensor_scalar", dtmp[0:1, :], dtmp[0:1, :], 0.9, None, ALU.max)
            W_("gpsimd", "sDVE", dm)
            def fmin(e):
                e.tensor_scalar(dvr[0:1, :], dtmp[0:1, :], 1.0, None,
                                ALU.min).then_inc(sems["sCK"], 1)
            plan.emit("gpsimd", fmin, meta=("inc", {"sCK": 1}, "ts"))
            ckm = plan.bump("sCK", 1)
        if SETUP_SUB >= 2:
            W_("tensor", "sCK", ckm)
            incs, dvm = pe_inc()
            mm(PA[:, 0:2], onesrow[:], dvr[0:1, 0:2], True, True, incs=incs)
            W_("vector", "sPE", dvm)
            dve_i("tensor_copy", colsC[:, 0:2], PA[:, 0:2])

        for p in range(NPASS):
            if p > 0:
                W_("sync", "sDVE", plan.c["sDVE"])
            for k2 in range(NK):
                dma("sync", x_t(k2),
                    d_xT[128 * k2:128 * (k2 + 1), TP * p:TP * (p + 1)])
            xm = plan.c["sX"]
            W_("tensor", "sX", xm)
            W_("vector", "sX", xm)
            if stage & 1:
                ffn_call()
            if stage & 2:
                attn()
            if stage & 4:
                ffn_call()
            W_("sync", "sDVE", plan.c["sDVE"])
            for k2 in range(NK):
                dma("sync", d_yT[128 * k2:128 * (k2 + 1), TP * p:TP * (p + 1)],
                    x_t(k2))
        W_("sync", "sX", plan.c["sX"])

        check_deadlock(plan)

        def replay(eng_name):
            def run(e):
                for item in plan.ops[eng_name]:
                    item(e)
            return run

        block.sync(replay("sync"))
        block.gpsimd(replay("gpsimd"))
        block.tensor(replay("tensor"))
        block.scalar(replay("scalar"))
        block.vector(replay("vector"))

    return nc


def _pack_expert_tiles(gu, dn):
    A = gu.reshape(12, 128, 16, 128)
    T = A.transpose(0, 2, 3, 1)
    order = []
    for i in range(6):
        order.append(T[i])
        order.append(T[6 + i])
    G = np.concatenate(order, axis=0).reshape(192, 128, 128)
    B = dn.reshape(16, 128, 6, 128).transpose(0, 2, 3, 1).reshape(96, 128, 128)
    allt = np.concatenate([G, B], axis=0)
    return allt.transpose(1, 0, 2).reshape(128, -1)


def _pack_shared_tiles(sgw, suw, sdw):
    out = []
    for j in range(SPN):
        o = SPW * j
        g = sgw[o:o + SPW].reshape(4, 128, 16, 128).transpose(0, 2, 3, 1)
        u = suw[o:o + SPW].reshape(4, 128, 16, 128).transpose(0, 2, 3, 1)
        order = []
        for i in range(4):
            order.append(g[i])
            order.append(u[i])
        G = np.concatenate(order, axis=0).reshape(128, 128, 128)
        B = sdw[:, o:o + SPW].reshape(16, 128, 4, 128)
        B = B.transpose(0, 2, 3, 1).reshape(64, 128, 128)
        out.append(np.concatenate([G, B], axis=0))
    allt = np.concatenate(out, axis=0)
    return allt.transpose(1, 0, 2).reshape(128, -1)


_NC_CACHE = {}


def kernel(L0_attn_x, L0_attn_weight, L0_attn_bias, L0_attn_decay_value,
           L0_ffn_gate_w, L0_ffn_gate_up_proj, L0_ffn_down_proj,
           L0_ffn_shared_gate_w, L0_ffn_shared_up_w, L0_ffn_shared_down_w,
           L0_ffn_shared_expert_gate_w, stage=7):
    x = np.asarray(L0_attn_x, dtype=np.float32)
    wt = np.asarray(L0_attn_weight, dtype=np.float32)
    bias = np.asarray(L0_attn_bias, dtype=np.float32)
    dv = np.asarray(L0_attn_decay_value, dtype=np.float32)
    gw = np.asarray(L0_ffn_gate_w, dtype=np.float32)
    gup = np.asarray(L0_ffn_gate_up_proj, dtype=np.float32)
    dnp_ = np.asarray(L0_ffn_down_proj, dtype=np.float32)
    sgw = np.asarray(L0_ffn_shared_gate_w, dtype=np.float32)
    suw = np.asarray(L0_ffn_shared_up_w, dtype=np.float32)
    sdw = np.asarray(L0_ffn_shared_down_w, dtype=np.float32)
    segw = np.asarray(L0_ffn_shared_expert_gate_w, dtype=np.float32)

    wexp = np.ascontiguousarray(np.concatenate(
        [_pack_expert_tiles(gup[e], dnp_[e]) for e in range(NE)], axis=1))
    wsh = np.ascontiguousarray(_pack_shared_tiles(sgw, suw, sdw))
    gwt = np.ascontiguousarray(
        gw.T.reshape(16, 128, 32).transpose(1, 0, 2).reshape(128, 16 * 32))
    segt = np.ascontiguousarray(segw.reshape(16, 128).T)
    biasc = np.ascontiguousarray(bias.reshape(16, 128).T)
    w0row = np.ascontiguousarray(wt[0:1, :])
    w1row = np.ascontiguousarray(wt[1:2, :])
    d0 = np.ascontiguousarray(dv[0:1, :])
    d1 = np.ascontiguousarray(dv[1:2, :])
    onescol = np.ones((128, 1), dtype=np.float32)
    onesrow = np.ones((1, 128), dtype=np.float32)
    onesrow16 = np.ones((1, 128), dtype=np.float16)
    ident = np.eye(128, dtype=np.float32)
    ident16 = np.eye(128, dtype=np.float16)

    key = ("v1", stage)
    if key not in _NC_CACHE:
        _NC_CACHE[key] = build_nc(stage)
    nc = _NC_CACHE[key]

    in_maps = []
    for c in range(NCORES):
        xT = np.ascontiguousarray(x[0, TCORE * c:TCORE * (c + 1), :].T)
        in_maps.append({
            "xT": xT, "wexp": wexp, "wsh": wsh, "gwt": gwt, "segt": segt,
            "biasc": biasc, "w0row": w0row, "w1row": w1row, "d0": d0, "d1": d1,
            "onescol": onescol, "onesrow": onesrow, "onesrow16": onesrow16,
            "ident": ident, "ident16": ident16,
        })
    res = run_bass_kernel_spmd(nc, in_maps, core_ids=list(range(NCORES)))
    parts = [np.asarray(res.results[c]["yT"]).T for c in range(NCORES)]
    y = np.concatenate(parts, axis=0)[None, :, :].astype(np.float32)
    return y


# version 06 (final)
# speedup vs baseline: 1.0831x; 1.0831x over previous
import numpy as np
from contextlib import ExitStack

import concourse.bass as bass
import concourse.mybir as mybir
from concourse.bass_utils import run_bass_kernel_spmd

F32 = mybir.dt.float32
F32R = mybir.dt.float32r
F16 = mybir.dt.float16
ALU = mybir.AluOpType
AF = mybir.ActivationFunctionType
AX = mybir.AxisListType
ET = mybir.EngineType

ATTN_SUB = 5
LN_SUB = 4
SETUP_SUB = 2
BARRIER_OFF = 0
DBG = 0

NCORES = 8
D = 2048
NK = 16
TCORE = 2048
TP = 1024
NPASS = 2
NE = 32
EPAIRS = 6
EDK = 6
ETILES = EPAIRS * 32 + 16 * EDK
CH = 16
ECH = ETILES // CH
SPW = 512
SPN = 8
SPAIRS = SPW // 128
SDK = SPW // 128
STILES = SPAIRS * 32 + 16 * SDK
SCH = STILES // CH
NCB = 6

SEMS = ("sX", "sW", "sCK", "sCB", "sPE", "sACT", "sDVE")
ENGS = ("sync", "gpsimd", "tensor", "scalar", "vector")


class Plan:
    def __init__(self):
        self.ops = {e: [] for e in ENGS}
        self.meta = {e: [] for e in ENGS}
        self.c = {s: 0 for s in SEMS}
        self.gchunk = 0
        self.attn_mark = 0
        self.chunk_release = {}
        self.pending_chunks = []
        self.crow_release = []
        self.dbg_done = False
        self.dbg_gate = 0

    def emit(self, eng, fn, meta=None):
        self.ops[eng].append(fn)
        self.meta[eng].append(meta or ("op",))

    def bump(self, sem, n):
        self.c[sem] += n
        return self.c[sem]


def check_deadlock(plan):
    pos = {e: 0 for e in ENGS}
    cnt = {s: 0 for s in SEMS}
    progress = True
    while progress:
        progress = False
        for e in ENGS:
            while pos[e] < len(plan.meta[e]):
                m = plan.meta[e][pos[e]]
                if m[0] == "wait":
                    if cnt[m[1]] < m[2]:
                        break
                elif m[0] == "inc":
                    for s, n in m[1].items():
                        cnt[s] += n
                pos[e] += 1
                progress = True
    stuck = {e: (pos[e], len(plan.meta[e])) for e in ENGS if pos[e] < len(plan.meta[e])}
    if stuck:
        lines = []
        for e, (p, n) in stuck.items():
            m = plan.meta[e][p]
            lines.append(f"{e} stuck at {p}/{n}: {m}; cnt={{{m[1]}: {cnt.get(m[1]) if m[0]=='wait' else '?'}}}")
        raise RuntimeError("DEADLOCK in planned schedule:\n" + "\n".join(lines))
    return cnt


def build_nc(stage=7):
    nc = bass.Bass("TRN2", target_bir_lowering=False, debug=False)
    nc.detect_race_conditions = False

    def din(name, shape, dt):
        return nc.dram_tensor(name, list(shape), dt, kind="ExternalInput").ap()

    d_xT = din("xT", (D, TCORE), F32R)
    d_wexp = din("wexp", (128, NE * ETILES * 128), F32)
    d_wsh = din("wsh", (128, SPN * STILES * 128), F32)
    d_gwt = din("gwt", (128, NK * NE), F32)
    d_segt = din("segt", (128, NK), F32)
    d_biasc = din("biasc", (128, NK), F32)
    d_w0row = din("w0row", (1, D), F32R)
    d_w1row = din("w1row", (1, D), F32R)
    d_d0 = din("d0", (1, 1), F32)
    d_d1 = din("d1", (1, 1), F32)
    d_onescol = din("onescol", (128, 1), F32R)
    d_onesrow = din("onesrow", (1, 128), F32R)
    d_ident = din("ident", (128, 128), F32)
    d_ident16 = din("ident16", (128, 128), F16)
    d_onesrow16 = din("onesrow16", (1, 128), F16)
    d_yT = nc.dram_tensor("yT", [D, TCORE], F32R, kind="ExternalOutput").ap()
    d_dbg = nc.dram_tensor("dbg", [128, 2048], F32, kind="ExternalOutput").ap()

    plan = Plan()

    with ExitStack() as st:
        def sbt(name, shape, dt):
            return st.enter_context(nc.sbuf_tensor(name, list(shape), dt))

        xt = sbt("xt", (128, NK * TP), F32R)
        ub = sbt("ub", (128, NK * TP), F16)
        arena = sbt("arena", (128, 8192), F32R)
        cbufs = sbt("cbufs", (128, 6 * 2048), F16)
        arenaF = sbt("arenaF", (128, 2048), F32)
        h = sbt("h", (128, 6144), F16)
        combT = sbt("combT", (32, TP), F16)
        crow = sbt("crow", (1, 2 * TP), F16)
        scale_bc = sbt("scale_bc", (128, TP), F32)
        gwt = sbt("gwt_s", (128, NK * NE), F16)
        segt = sbt("segt_s", (128, NK), F16)
        biasc = sbt("biasc_s", (128, NK), F32)
        onescol = sbt("onescol_s", (128, 1), F32R)
        onesrow = sbt("onesrow_s", (1, 128), F32R)
        onesrow16 = sbt("onesrow16_s", (1, 128), F16)
        ident = sbt("ident_s", (128, 128), F32)
        ident16 = sbt("ident16_s", (128, 128), F16)
        colsA = sbt("colsA", (128, 8), F32)
        colsB = sbt("colsB", (128, 8), F32)
        colsC = sbt("colsC", (128, 2), F32)
        colsD = sbt("colsD", (128, 8), F32)
        colsS = sbt("colsS", (128, 8), F32)
        dvr = sbt("dvr", (1, 2), F32R)
        dtmp = sbt("dtmp", (1, 2), F32)

        PA = st.enter_context(nc.psum_tensor("PA", [128, 1024], F32))
        PB = st.enter_context(nc.psum_tensor("PB", [128, 1024], F32))
        PC = st.enter_context(nc.psum_tensor("PC", [128, 1024], F32))
        PD = st.enter_context(nc.psum_tensor("PD", [128, 1024], F32))
        PS = [PA, PB, PC, PD]

        sems = {s: st.enter_context(nc.semaphore(s)) for s in SEMS}

        block = st.enter_context(nc.Block())

        def cbuf(b):
            return cbufs[:, 2048 * b:2048 * (b + 1)]
        def xsq(j):
            return arena[:, 6144 + 1024 * j: 6144 + 1024 * (j + 1)]
        def sg(j):
            return arenaF[:, 1024 * j:1024 * (j + 1)]
        ut_cur = arena[:, 0:2048]
        w0bc = arena[:, 2048:4096]
        w1bc = arena[:, 4096:6144]
        yF2 = arenaF[:, 0:2048]
        ltm = arenaF[:, 0:512]
        msk = arenaF[:, 512:544]
        comb_tm = arenaF[:, 544:800]
        rowA = arenaF[0:1, 0:1024]
        rowA2 = arenaF[0:1, 1024:2048]

        def hk(k):
            return h[:, 1024 * k:1024 * (k + 1)]
        lg = h[:, 0:1024]
        m_row = crow[0:1, 1024:2048]
        w0row_sb = arena[0:1, 2048:4096]
        w1row_sb = arena[0:1, 4096:6144]
        rinv_row = crow[0:1, 0:1024]
        recip_tmp = scale_bc[0:1, 0:1024]

        def x_t(m):
            return xt[:, TP * m:TP * (m + 1)]
        def u_t(k):
            return ub[:, TP * k:TP * (k + 1)]

        def W_(eng, sem, v):
            plan.emit(eng, lambda e, sem=sem, v=v: e.wait_ge(sems[sem], v),
                      meta=("wait", sem, v))

        def mm(out_ap, lhs_ap, rhs_ap, start, stop, incs=()):
            def f(e, out_ap=out_ap, lhs_ap=lhs_ap, rhs_ap=rhs_ap,
                  start=start, stop=stop, incs=tuple(incs)):
                ins = e.matmul(out_ap, lhs_ap, rhs_ap, start=start, stop=stop)
                for s, n in incs:
                    ins = ins.then_inc(sems[s], n)
            plan.emit("tensor", f, meta=("inc", dict(incs), "mm"))

        def trans(out_ap, in_ap, id_ap, incs=()):
            def f(e, out_ap=out_ap, in_ap=in_ap, id_ap=id_ap, incs=tuple(incs)):
                ins = e.transpose(out_ap, in_ap, id_ap)
                for s, n in incs:
                    ins = ins.then_inc(sems[s], n)
            plan.emit("tensor", f, meta=("inc", dict(incs), "trans"))

        def pe_inc():
            v = plan.bump("sPE", 1)
            return [("sPE", 1)], v

        def dve(opname, *args, incs=()):
            def f(e, opname=opname, args=args, incs=tuple(incs)):
                try:
                    ins = getattr(e, opname)(*args)
                except Exception as ex:
                    raise RuntimeError(f"DVE op {opname} failed: {ex}") from ex
                for s, n in incs:
                    ins = ins.then_inc(sems[s], n)
            plan.emit("vector", f, meta=("inc", dict(incs), opname))

        def dve_i(opname, *args):
            dve(opname, *args, incs=[("sDVE", 1)])
            return plan.bump("sDVE", 1)

        def act(opname, *args, incs=(), kwargs=None):
            def f(e, opname=opname, args=args, incs=tuple(incs), kwargs=kwargs or {}):
                ins = getattr(e, opname)(*args, **kwargs)
                for s, n in incs:
                    ins = ins.then_inc(sems[s], n)
            plan.emit("scalar", f, meta=("inc", dict(incs), opname))

        def dma(eng, out_ap, in_ap):
            sem = "sX" if eng == "sync" else "sW"
            def f(e, out_ap=out_ap, in_ap=in_ap, sem=sem):
                e.dma_start(out=out_ap, in_=in_ap).then_inc(sems[sem], 16)
            plan.emit(eng, f, meta=("inc", {sem: 16}, "dma"))
            return plan.bump(sem, 16)

        def barrier():
            if BARRIER_OFF:
                return
            sem_of = {"tensor": "sPE", "scalar": "sACT", "vector": "sDVE"}
            tgt = {s: plan.c[s] for s in sem_of.values()}
            for eng, s in sem_of.items():
                for o in sem_of.values():
                    if o != s:
                        W_(eng, o, tgt[o])

        def stream_chunks(src_ap, g0, n_chunks, gate_first=None):
            for j in range(n_chunks):
                g = g0 + j
                if g >= NCB:
                    W_("gpsimd", "sPE", plan.chunk_release[g - (NCB - 1)])
                if j < NCB and gate_first is not None:
                    W_("gpsimd", gate_first[0], gate_first[1])
                dma("gpsimd", cbuf(g % NCB)[:],
                    src_ap[:, 2048 * j:2048 * (j + 1)])

        def layernorm():
            if LN_SUB < 1:
                return
            marks = []
            for k in range(NK):
                if k >= 2:
                    W_("vector", "sPE", marks[k - 2])
                dm = dve_i("tensor_tensor", xsq(k % 2)[:], x_t(k), x_t(k), ALU.mult)
                W_("tensor", "sDVE", dm)
                for n in range(2):
                    mm(PA[0:1, 512 * n:512 * (n + 1)], onescol[:],
                       x_t(k)[:, 512 * n:512 * (n + 1)], k == 0, k == NK - 1)
                incs, mk = pe_inc()
                marks.append(mk)
                for n in range(2):
                    mm(PB[0:1, 512 * n:512 * (n + 1)], onescol[:],
                       xsq(k % 2)[:, 512 * n:512 * (n + 1)], k == 0, k == NK - 1,
                       incs=incs if n == 1 else ())
            if LN_SUB < 2:
                return
            W_("vector", "sPE", plan.c["sPE"])
            dve("tensor_scalar", m_row[:, 0:512], PA[0:1, 0:512], 1.0 / D, None, ALU.mult)
            dve("tensor_scalar", m_row[:, 512:1024], PA[0:1, 512:1024], 1.0 / D, None, ALU.mult)
            dve("tensor_scalar", rowA[:], PB[0:1, :], 1.0 / D, None, ALU.mult)
            dve("tensor_tensor", rowA2[:], m_row[:], m_row[:], ALU.mult)
            dve("tensor_sub", rowA[:], rowA[:], rowA2[:])
            dm = dve_i("tensor_scalar", rowA[:], rowA[:], 1e-5, None, ALU.add)
            W_("scalar", "sDVE", dm)
            act("activation", rowA2[:], rowA[:], AF.Sqrt, incs=[("sACT", 1)])
            am = plan.bump("sACT", 1)
            W_("vector", "sACT", am)
            dve("reciprocal", scale_bc[0:1, 0:1024], rowA2[:])
            dm = dve_i("tensor_copy", rinv_row[:], scale_bc[0:1, 0:1024])
            W_("tensor", "sDVE", dm)
            if LN_SUB < 3:
                return
            for n in range(2):
                mm(PC[:, 512 * n:512 * (n + 1)], onesrow16[:],
                   m_row[:, 512 * n:512 * (n + 1)], True, True)
            incs, bcm = pe_inc()
            for n in range(2):
                mm(PD[:, 512 * n:512 * (n + 1)], onesrow16[:],
                   rinv_row[:, 512 * n:512 * (n + 1)], True, True,
                   incs=incs if n == 1 else ())
            if LN_SUB == 3:
                W_("vector", "sPE", bcm)
                dve_i("tensor_copy", u_t(0)[:, 0:512], PA[:, 0:512])
                return
            if LN_SUB == 7:
                W_("vector", "sPE", bcm)
                dve_i("tensor_copy", u_t(0), PC[:])
                return
            W_("vector", "sPE", bcm)
            for k in range(NK):
                dve("tensor_sub", u_t(k), x_t(k), PC[:])
                dve_i("tensor_tensor", u_t(k), u_t(k), PD[:], ALU.mult)

        def router():
            W_("tensor", "sDVE", plan.c["sDVE"])
            W_("tensor", "sW", 32)
            incs, lgm = pe_inc()
            for k in range(NK):
                for n in range(2):
                    last = (k == NK - 1 and n == 1)
                    mm(PA[0:32, 512 * n:512 * (n + 1)], gwt[:, 32 * k:32 * (k + 1)],
                       u_t(k)[:, 512 * n:512 * (n + 1)], k == 0, k == NK - 1,
                       incs=incs if last else ())
            W_("vector", "sPE", lgm)
            dm = dve_i("tensor_copy", lg[0:32, :], PA[0:32, :])
            W_("tensor", "sDVE", dm)
            incs, trm = pe_inc()
            for c in range(8):
                trans(PB[:, 16 * c:16 * (c + 1)].bitcast(F16),
                      lg[0:32, 128 * c:128 * (c + 1)],
                      ident16[0:32, 0:32], incs=incs if c == 7 else ())
            L3 = ltm[:, 0:256].rearrange("p (c e) -> p c e", e=32)
            C3 = comb_tm[:, 0:256].rearrange("p (c e) -> p c e", e=32)
            M3 = arenaF[:, 800:1056].rearrange("p (c e) -> p c e", e=32)
            cA = colsA[:, 0:8]
            def bc3(cols):
                return cols[:, :, None].broadcast_to([128, 8, 32])
            def pool(opname, *args, incs=()):
                def f(e, opname=opname, args=args, incs=tuple(incs)):
                    ins = getattr(e, opname)(*args)
                    for s, n in incs:
                        ins = ins.then_inc(sems[s], n)
                plan.emit("gpsimd", f, meta=("inc", dict(incs), opname))
            def pool_i(opname, *args):
                pool(opname, *args, incs=[("sCK", 1)])
                return plan.bump("sCK", 1)
            W_("scalar", "sPE", trm)
            act("copy", ltm[:, 0:256], PB[:, 0:128].bitcast(F16), incs=[("sACT", 1)])
            am = plan.bump("sACT", 1)
            W_("vector", "sACT", am)
            dm = dve_i("reduce_max", cA, L3, AX.X)
            W_("gpsimd", "sDVE", dm)
            pm = pool_i("tensor_tensor", C3, L3, bc3(cA), ALU.subtract)
            W_("scalar", "sCK", pm)
            act("activation", ltm[:, 0:256], comb_tm[:, 0:256], AF.Exp,
                incs=[("sACT", 1)])
            am = plan.bump("sACT", 1)
            W_("vector", "sACT", am)
            dm = dve_i("reduce_sum", colsS[:, 0:8], L3, AX.X)
            W_("gpsimd", "sDVE", dm)
            pm = pool_i("tensor_copy", colsB[:, 0:8], colsS[:, 0:8])
            W_("vector", "sCK", pm)
            dm = dve_i("reciprocal", colsD[:, 0:8], colsB[:, 0:8])
            W_("gpsimd", "sDVE", dm)
            pm = pool_i("tensor_tensor", C3, L3, bc3(colsD[:, 0:8]), ALU.mult)
            if DBG and not plan.dbg_done:
                W_("vector", "sCK", pm)
                dve("tensor_copy", scale_bc[:, 0:256], comb_tm[:, 0:256])
            for j in range(4):
                src3 = C3 if j == 0 else L3
                W_("vector", "sCK", plan.c["sCK"])
                dm = dve_i("reduce_max", cA, src3, AX.X)
                W_("gpsimd", "sDVE", dm)
                pm = pool_i("tensor_copy", colsB[:, 0:8], cA)
                if j == 3:
                    break
                W_("vector", "sCK", pm)
                dve("tensor_tensor", M3, src3, bc3(colsB[:, 0:8]), ALU.is_ge)
                dm = dve_i("scalar_tensor_tensor", L3, M3, -1e30, src3,
                           ALU.mult, ALU.add)
            W_("vector", "sCK", pm)
            dve("tensor_tensor", M3, C3, bc3(colsB[:, 0:8]), ALU.is_ge)
            dm = dve_i("tensor_tensor", C3, C3, M3, ALU.mult)
            W_("vector", "sDVE", dm)
            dm = dve_i("reduce_sum", colsS[:, 0:8], C3, AX.X)
            W_("gpsimd", "sDVE", dm)
            pm = pool_i("tensor_copy", colsB[:, 0:8], colsS[:, 0:8])
            W_("vector", "sCK", pm)
            dm = dve_i("reciprocal", colsD[:, 0:8], colsB[:, 0:8])
            W_("gpsimd", "sDVE", dm)
            pm = pool_i("tensor_copy", colsB[:, 0:8], colsD[:, 0:8])
            W_("vector", "sCK", pm)
            dve("tensor_tensor", C3, C3, bc3(colsB[:, 0:8]), ALU.mult)
            dm = dve_i("tensor_scalar", comb_tm[:, 0:256], comb_tm[:, 0:256],
                       0.5, None, ALU.mult)
            W_("tensor", "sDVE", dm)
            incs, ctm = pe_inc()
            for c in range(8):
                trans(PC[0:32, 128 * c:128 * (c + 1)],
                      comb_tm[:, 32 * c:32 * (c + 1)],
                      ident[:, :], incs=incs if c == 7 else ())
            W_("vector", "sPE", ctm)
            return dve_i("tensor_copy", combT[0:32, :], PC[0:32, :])

        def shared_gate():
            W_("tensor", "sDVE", plan.c["sDVE"])
            incs, sgm = pe_inc()
            for k in range(NK):
                for n in range(2):
                    last = (k == NK - 1 and n == 1)
                    mm(PD[0:1, 512 * n:512 * (n + 1)], segt[:, k:k + 1],
                       u_t(k)[:, 512 * n:512 * (n + 1)], k == 0, k == NK - 1,
                       incs=incs if last else ())
            W_("scalar", "sPE", sgm)
            act("activation", rowA[:], PD[0:1, :], AF.Sigmoid, incs=[("sACT", 1)])
            am = plan.bump("sACT", 1)
            W_("vector", "sACT", am)
            dm = dve_i("tensor_scalar", m_row[:], rowA[:], 0.5, None, ALU.mult)
            W_("tensor", "sDVE", dm)
            incs, bcm = pe_inc()
            for n in range(2):
                mm(PB[:, 512 * n:512 * (n + 1)], onesrow16[:],
                   m_row[:, 512 * n:512 * (n + 1)], True, True,
                   incs=incs if n == 1 else ())
            W_("vector", "sPE", bcm)
            if DBG:
                W_("vector", "sX", plan.dbg_gate)
            dve_i("tensor_copy", scale_bc[:], PB[:])

        class Ev:
            pass

        def expert_block(ev, pairs, dkt, do_comb, crow_mark, pe_tile, acc=None):
            if do_comb:
                W_("tensor", "sX", crow_mark)
                W_("tensor", ev.PB_free[0], ev.PB_free[1])
                incs, cm = pe_inc()
                for n in range(2):
                    mm(PB[:, 512 * n:512 * (n + 1)], onesrow16[:],
                       crow[0:1, 512 * n:512 * (n + 1)], True, True,
                       incs=incs if n == 1 else ())
                plan.crow_release.append(cm)
                W_("vector", "sPE", cm)
                ev.PB_free = ("sDVE", dve_i("tensor_copy", scale_bc[:], PB[:]))

            def tile_ap(t):
                g = pe_tile[0] + t
                c = g // CH
                return cbuf(c % NCB)[:, 128 * (g % CH):128 * (g % CH) + 128], c, \
                    (g % CH == CH - 1)

            def release_pending(pe_val):
                for g in plan.pending_chunks:
                    plan.chunk_release[g] = pe_val
                plan.pending_chunks = []

            hready = None
            for p in range(pairs):
                W_("tensor", ev.PA_free[0], ev.PA_free[1])
                incs_g, gmark = pe_inc()
                for k in range(NK):
                    ap, c, islast = tile_ap(p * 32 + k)
                    if (p * 32 + k + pe_tile[0]) % CH == 0:
                        W_("tensor", "sW", 32 + 16 * (c + 1))
                    for n in range(2):
                        extra = incs_g if (k == NK - 1 and n == 1) else []
                        mm(PA[:, 512 * n:512 * (n + 1)], ap,
                           u_t(k)[:, 512 * n:512 * (n + 1)], k == 0, k == NK - 1,
                           incs=extra)
                    if islast:
                        plan.pending_chunks.append(c)
                release_pending(gmark)
                W_("scalar", "sPE", gmark)
                W_("scalar", "sDVE", ev.sg_free[p % 2])
                act("activation", sg(p % 2)[:], PA[:], AF.Silu, incs=[("sACT", 1)])
                ev.PA_free = ("sACT", plan.bump("sACT", 1))
                W_("vector", "sACT", plan.c["sACT"])
                dve("tensor_tensor", sg(p % 2)[:], sg(p % 2)[:], scale_bc[:], ALU.mult)
                W_("tensor", ev.PB_free[0], ev.PB_free[1])
                incs_u, umark = pe_inc()
                for k in range(NK):
                    ap, c, islast = tile_ap(p * 32 + 16 + k)
                    if (p * 32 + 16 + k + pe_tile[0]) % CH == 0:
                        W_("tensor", "sW", 32 + 16 * (c + 1))
                    for n in range(2):
                        extra = incs_u if (k == NK - 1 and n == 1) else []
                        mm(PB[:, 512 * n:512 * (n + 1)], ap,
                           u_t(k)[:, 512 * n:512 * (n + 1)], k == 0, k == NK - 1,
                           incs=extra)
                    if islast:
                        plan.pending_chunks.append(c)
                release_pending(umark)
                W_("vector", "sPE", umark)
                hm = dve_i("tensor_tensor", hk(p), PB[:], sg(p % 2)[:], ALU.mult)
                ev.PB_free = ("sDVE", hm)
                ev.sg_free[p % 2] = hm
                hready = hm
            W_("tensor", "sDVE", hready)
            for m in range(NK):
                reg = m % 2
                P_ = PC if reg == 0 else PD
                if acc != 1:
                    W_("tensor", ev.PCD_free[reg][0], ev.PCD_free[reg][1])
                incs_d, dmark = pe_inc()
                for k in range(dkt):
                    ap, c, islast = tile_ap(pairs * 32 + m * dkt + k)
                    if (pairs * 32 + m * dkt + k + pe_tile[0]) % CH == 0:
                        W_("tensor", "sW", 32 + 16 * (c + 1))
                    for n in range(2):
                        extra = incs_d if (k == dkt - 1 and n == 1) else []
                        mm(P_[:, 512 * n:512 * (n + 1)], ap,
                           hk(k)[:, 512 * n:512 * (n + 1)],
                           (k == 0 and acc != 1),
                           (k == dkt - 1 and acc != 0),
                           incs=extra)
                    if islast:
                        plan.pending_chunks.append(c)
                release_pending(dmark)
                if acc != 0:
                    W_("vector", "sPE", dmark)
                    dm = dve_i("scalar_tensor_tensor", x_t(m), P_[:], 1.0, x_t(m),
                               ALU.mult, ALU.add)
                    ev.PCD_free[reg] = ("sDVE", dm)
            pe_tile[0] += pairs * 32 + NK * dkt

        def fresh_ev():
            ev = Ev()
            ev.PA_free = ("sACT", plan.c["sACT"])
            ev.PB_free = ("sDVE", plan.c["sDVE"])
            ev.PCD_free = [("sDVE", plan.c["sDVE"]), ("sDVE", plan.c["sDVE"])]
            ev.sg_free = {0: plan.c["sDVE"], 1: plan.c["sDVE"]}
            return ev

        def ffn_call():
            barrier()
            layernorm()
            barrier()
            combT_mark = router()
            if DBG and not plan.dbg_done:
                plan.dbg_done = True
                W_("sync", "sDVE", combT_mark)
                dma("sync", d_dbg.bitcast(F32R)[0:32, 0:1024], combT[0:32, :])
                sxm = dma("sync", d_dbg[:, 1024:2048], scale_bc[:, 0:1024])
                plan.dbg_gate = sxm
            barrier()
            shared_gate()
            barrier()
            ch0 = plan.gchunk
            plan.gchunk += SPN * SCH
            ev = fresh_ev()
            pe_tile = [ch0 * CH]
            for j in range(SPN):
                expert_block(ev, SPAIRS, SDK, False, None, pe_tile)
            stream_chunks(d_wsh, ch0, SPN * SCH)
            barrier()
            ch1 = plan.gchunk
            plan.gchunk += NE * ECH
            crow_base = len(plan.crow_release)
            xB = plan.c["sX"]
            crow_marks = [xB + 16 * (e_ + 1) for e_ in range(NE)]
            for e_ in range(NE):
                plan.bump("sX", 16)
            ev = fresh_ev()
            pe_tile = [ch1 * CH]
            for e_ in range(NE):
                expert_block(ev, EPAIRS, EDK, True, crow_marks[e_], pe_tile)
            for e_ in range(NE):
                W_("sync", "sDVE", combT_mark)
                if e_ > 0:
                    W_("sync", "sPE", plan.crow_release[crow_base + e_ - 1])
                def f(e, e_=e_):
                    e.dma_start(out=crow[0:1, 0:1024], in_=combT[e_:e_ + 1, :]).then_inc(
                        sems["sX"], 16)
                plan.emit("sync", f, meta=("inc", {"sX": 16}, "dma"))
            stream_chunks(d_wexp, ch1, NE * ECH)
            barrier()

        def slot(d_):
            P_ = PS[(d_ % 8) // 2]
            off = 512 * (d_ % 2) + 128 * (d_ // 8)
            return P_[:, off:off + 128]

        def slot16(d_):
            P_ = PS[(d_ % 8) // 2]
            off = 512 * (d_ % 2) + 64 * (d_ // 8)
            return P_[:, off:off + 64].bitcast(F16)

        def attn():
            if ATTN_SUB < 1:
                return
            barrier()
            layernorm()
            barrier()
            if ATTN_SUB < 2:
                plan.attn_mark = plan.c["sDVE"]
                return
            W_("sync", "sDVE", plan.c["sDVE"])
            dma("sync", w0row_sb[:], d_w0row)
            r2 = dma("sync", w1row_sb[:], d_w1row)
            W_("tensor", "sX", r2)
            incs, bm = pe_inc()
            for n in range(4):
                mm((PA if n < 2 else PB)[:, 512 * (n % 2):512 * (n % 2) + 512],
                   onesrow[:], w0row_sb[:, 512 * n:512 * (n + 1)], True, True)
            for n in range(4):
                mm((PC if n < 2 else PD)[:, 512 * (n % 2):512 * (n % 2) + 512],
                   onesrow[:], w1row_sb[:, 512 * n:512 * (n + 1)], True, True,
                   incs=incs if n == 3 else ())
            W_("vector", "sPE", bm)
            dve("tensor_copy", w0bc[:, 0:1024], PA[:])
            dve("tensor_copy", w0bc[:, 1024:2048], PB[:])
            dve("tensor_copy", w1bc[:, 0:1024], PC[:])
            dm = dve_i("tensor_copy", w1bc[:, 1024:2048], PD[:])
            W_("tensor", "sDVE", dm)
            if ATTN_SUB < 3:
                plan.attn_mark = plan.c["sDVE"]
                barrier()
                return
            dv0b = colsC[:, 0:1].broadcast_to([128, 2048])
            dv1b = colsC[:, 1:2].broadcast_to([128, 2048])
            scX = arena[:, 6144:8192]
            evs2 = {}
            for tt in range(8):
                tok0 = 128 * tt
                evs = {}
                for d_ in range(NK):
                    if d_ >= 8:
                        W_("tensor", "sDVE", evs[d_ - 8])
                    elif tt > 0:
                        W_("tensor", "sDVE", evs2[d_ + 8])
                    incs, mk = pe_inc()
                    trans(slot16(d_), u_t(d_)[:, tok0:tok0 + 128],
                          ident16[:, :], incs=incs)
                    W_("vector", "sPE", mk)
                    evs[d_] = dve_i("tensor_copy",
                                    ut_cur[:, 128 * d_:128 * (d_ + 1)], slot16(d_))
                if ATTN_SUB < 4:
                    evs2 = {k: v for k, v in evs.items()}
                    continue
                dve("tensor_tensor", scX, ut_cur[:].bitcast(F32),
                    w0bc[:].bitcast(F32), ALU.mult)
                dve("tensor_tensor_scan", yF2[:], dv0b, scX.bitcast(F32),
                    0.0, ALU.mult, ALU.add)
                dve("tensor_copy", scX, yF2[:])
                dve("tensor_tensor_scan", yF2[:], dv1b, ut_cur[:].bitcast(F32),
                    0.0, ALU.mult, ALU.add)
                dve("tensor_tensor", yF2[:], yF2[:], w1bc[:].bitcast(F32), ALU.mult)
                ysc = dve_i("tensor_tensor", yF2[:], yF2[:], scX.bitcast(F32),
                            ALU.add)
                W_("tensor", "sDVE", ysc)
                if ATTN_SUB < 5:
                    evs2 = {k: v for k, v in evs.items()}
                    continue
                evs2 = {}
                for d_ in range(NK):
                    if d_ >= 8:
                        W_("tensor", "sDVE", evs2[d_ - 8])
                    incs, mk = pe_inc()
                    trans(slot(d_), yF2[:, 128 * d_:128 * (d_ + 1)],
                          ident[:, :], incs=incs)
                    W_("vector", "sPE", mk)
                    evs2[d_] = dve_i("scalar_tensor_tensor",
                                     x_t(d_)[:, tok0:tok0 + 128], slot(d_),
                                     biasc[:, d_:d_ + 1],
                                     x_t(d_)[:, tok0:tok0 + 128],
                                     ALU.add, ALU.add)
            plan.attn_mark = plan.c["sDVE"]
            barrier()

        for dst, src in [(biasc[:], d_biasc),
                         (onescol[:], d_onescol), (onesrow[:], d_onesrow),
                         (onesrow16[:], d_onesrow16),
                         (ident[:], d_ident), (ident16[:], d_ident16),
                         (dtmp[0:1, 0:1], d_d0), (dtmp[0:1, 1:2], d_d1)]:
            dma("sync", dst, src)
        setup_mark = plan.c["sX"]
        dma("gpsimd", gwt[:], d_gwt)
        gseg_mark = dma("gpsimd", segt[:], d_segt)
        if SETUP_SUB >= 1:
            W_("vector", "sX", setup_mark)
            dm = dve_i("tensor_scalar", dtmp[0:1, :], dtmp[0:1, :], 0.9, None, ALU.max)
            W_("gpsimd", "sDVE", dm)
            def fmin(e):
                e.tensor_scalar(dvr[0:1, :], dtmp[0:1, :], 1.0, None,
                                ALU.min).then_inc(sems["sCK"], 1)
            plan.emit("gpsimd", fmin, meta=("inc", {"sCK": 1}, "ts"))
            ckm = plan.bump("sCK", 1)
        if SETUP_SUB >= 2:
            W_("tensor", "sCK", ckm)
            incs, dvm = pe_inc()
            mm(PA[:, 0:2], onesrow[:], dvr[0:1, 0:2], True, True, incs=incs)
            W_("vector", "sPE", dvm)
            dve_i("tensor_copy", colsC[:, 0:2], PA[:, 0:2])

        for p in range(NPASS):
            if p > 0:
                W_("sync", "sDVE", plan.c["sDVE"])
            for k2 in range(NK):
                dma("sync", x_t(k2),
                    d_xT[128 * k2:128 * (k2 + 1), TP * p:TP * (p + 1)])
            xm = plan.c["sX"]
            W_("tensor", "sX", xm)
            W_("vector", "sX", xm)
            if stage & 1:
                ffn_call()
            if stage & 2:
                attn()
            if stage & 4:
                ffn_call()
            W_("sync", "sDVE", plan.c["sDVE"])
            for k2 in range(NK):
                dma("sync", d_yT[128 * k2:128 * (k2 + 1), TP * p:TP * (p + 1)],
                    x_t(k2))
        W_("sync", "sX", plan.c["sX"])

        check_deadlock(plan)

        def replay(eng_name):
            def run(e):
                for item in plan.ops[eng_name]:
                    item(e)
            return run

        block.sync(replay("sync"))
        block.gpsimd(replay("gpsimd"))
        block.tensor(replay("tensor"))
        block.scalar(replay("scalar"))
        block.vector(replay("vector"))

    return nc


def _pack_expert_tiles(gu, dn):
    A = gu.reshape(12, 128, 16, 128)
    T = A.transpose(0, 2, 3, 1)
    order = []
    for i in range(6):
        order.append(T[i])
        order.append(T[6 + i])
    G = np.concatenate(order, axis=0).reshape(192, 128, 128)
    B = dn.reshape(16, 128, 6, 128).transpose(0, 2, 3, 1).reshape(96, 128, 128)
    allt = np.concatenate([G, B], axis=0)
    return allt.transpose(1, 0, 2).reshape(128, -1)


def _pack_shared_tiles(sgw, suw, sdw):
    out = []
    for j in range(SPN):
        o = SPW * j
        g = sgw[o:o + SPW].reshape(4, 128, 16, 128).transpose(0, 2, 3, 1)
        u = suw[o:o + SPW].reshape(4, 128, 16, 128).transpose(0, 2, 3, 1)
        order = []
        for i in range(4):
            order.append(g[i])
            order.append(u[i])
        G = np.concatenate(order, axis=0).reshape(128, 128, 128)
        B = sdw[:, o:o + SPW].reshape(16, 128, 4, 128)
        B = B.transpose(0, 2, 3, 1).reshape(64, 128, 128)
        out.append(np.concatenate([G, B], axis=0))
    allt = np.concatenate(out, axis=0)
    return allt.transpose(1, 0, 2).reshape(128, -1)


_NC_CACHE = {}


def kernel(L0_attn_x, L0_attn_weight, L0_attn_bias, L0_attn_decay_value,
           L0_ffn_gate_w, L0_ffn_gate_up_proj, L0_ffn_down_proj,
           L0_ffn_shared_gate_w, L0_ffn_shared_up_w, L0_ffn_shared_down_w,
           L0_ffn_shared_expert_gate_w, stage=7):
    x = np.asarray(L0_attn_x, dtype=np.float32)
    wt = np.asarray(L0_attn_weight, dtype=np.float32)
    bias = np.asarray(L0_attn_bias, dtype=np.float32)
    dv = np.asarray(L0_attn_decay_value, dtype=np.float32)
    gw = np.asarray(L0_ffn_gate_w, dtype=np.float32)
    gup = np.asarray(L0_ffn_gate_up_proj, dtype=np.float32)
    dnp_ = np.asarray(L0_ffn_down_proj, dtype=np.float32)
    sgw = np.asarray(L0_ffn_shared_gate_w, dtype=np.float32)
    suw = np.asarray(L0_ffn_shared_up_w, dtype=np.float32)
    sdw = np.asarray(L0_ffn_shared_down_w, dtype=np.float32)
    segw = np.asarray(L0_ffn_shared_expert_gate_w, dtype=np.float32)

    eblocks = [_pack_expert_tiles(gup[e], dnp_[e]) for e in range(NE)]
    wsh0 = _pack_shared_tiles(sgw, suw, sdw)
    SHW = STILES * 128
    segt = np.ascontiguousarray(segw.reshape(16, 128).T)
    biasc = np.ascontiguousarray(bias.reshape(16, 128).T)
    w0row = np.ascontiguousarray(wt[0:1, :])
    w1row = np.ascontiguousarray(wt[1:2, :])
    d0 = np.ascontiguousarray(dv[0:1, :])
    d1 = np.ascontiguousarray(dv[1:2, :])
    onescol = np.ones((128, 1), dtype=np.float32)
    onesrow = np.ones((1, 128), dtype=np.float32)
    onesrow16 = np.ones((1, 128), dtype=np.float16)
    ident = np.eye(128, dtype=np.float32)
    ident16 = np.eye(128, dtype=np.float16)

    key = ("v1", stage)
    if key not in _NC_CACHE:
        _NC_CACHE[key] = build_nc(stage)
    nc = _NC_CACHE[key]

    in_maps = []
    for c in range(NCORES):
        xT = np.ascontiguousarray(x[0, TCORE * c:TCORE * (c + 1), :].T)
        perm = [(i + 4 * c) % NE for i in range(NE)]
        wexp = np.ascontiguousarray(
            np.concatenate([eblocks[p] for p in perm], axis=1))
        wsh = np.ascontiguousarray(np.concatenate(
            [wsh0[:, SHW * ((i + c) % SPN):SHW * ((i + c) % SPN) + SHW]
             for i in range(SPN)], axis=1))
        gwt = np.ascontiguousarray(
            gw[perm].T.reshape(16, 128, 32).transpose(1, 0, 2)
            .reshape(128, 16 * 32))
        in_maps.append({
            "xT": xT, "wexp": wexp, "wsh": wsh, "gwt": gwt, "segt": segt,
            "biasc": biasc, "w0row": w0row, "w1row": w1row, "d0": d0, "d1": d1,
            "onescol": onescol, "onesrow": onesrow, "onesrow16": onesrow16,
            "ident": ident, "ident16": ident16,
        })
    res = run_bass_kernel_spmd(nc, in_maps, core_ids=list(range(NCORES)))
    parts = [np.asarray(res.results[c]["yT"]).T for c in range(NCORES)]
    y = np.concatenate(parts, axis=0)[None, :, :].astype(np.float32)
    return y
